# version 11 (final)
# speedup vs baseline: 1.2048x; 1.0628x over previous
import numpy as np
import ml_dtypes

import concourse.bass as bass
import concourse.mybir as mybir
from concourse.bass_utils import run_bass_kernel_spmd

F32 = mybir.dt.float32
BF16 = mybir.dt.bfloat16
ALU = mybir.AluOpType
ACT = mybir.ActivationFunctionType
AX = mybir.AxisListType

B, T, D, DI, N, RK, FF = 8, 2048, 1024, 2048, 16, 64, 4096
NCT, ND, NF = DI // 128, D // 128, FF // 128
TT = 128
NTILE = T // TT
TC = 8
NSUB = TT // TC

MAGIC = 12582912.0
EM10 = float(np.float32(np.exp(-10.0)))
EPS = 1e-8
LN_EPS = 1e-5
NEG_BIG = -3.0e38
ARENA_W = 25152


class Plan:
    ENGINES = ("sync", "gpsimd", "vector", "scalar", "tensor")

    def __init__(self):
        self.q = {e: [] for e in self.ENGINES}
        self.cnt = {}

    def emit(self, eng, fn):
        self.q[eng].append(fn)

    def run(self, eng, h):
        for fn in self.q[eng]:
            fn(h)


def build_nc(dbg=(), phase="full"):
    nc = bass.Bass("TRN2", target_bir_lowering=False, detect_race_conditions=False)
    ap = lambda t: t.ap()

    x_tok = ap(nc.dram_tensor("x_tok", [T, D], F32, kind="ExternalInput"))
    x_feat = ap(nc.dram_tensor("x_feat", [D, T], F32, kind="ExternalInput"))
    w_in = ap(nc.dram_tensor("w_in", [D, 2 * DI], F32, kind="ExternalInput"))
    w_conv = ap(nc.dram_tensor("w_conv", [128, NCT * 4], F32, kind="ExternalInput"))
    b_conv = ap(nc.dram_tensor("b_conv", [128, NCT], F32, kind="ExternalInput"))
    w_xp = ap(nc.dram_tensor("w_xp", [DI, 96], F32, kind="ExternalInput"))
    w_dt = ap(nc.dram_tensor("w_dt", [RK, DI], F32, kind="ExternalInput"))
    b_dt = ap(nc.dram_tensor("b_dt", [128, NCT], F32, kind="ExternalInput"))
    a_log = ap(nc.dram_tensor("a_log", [128, NCT * N], F32, kind="ExternalInput"))
    d_vec = ap(nc.dram_tensor("d_vec", [128, NCT], F32, kind="ExternalInput"))
    w_out = ap(nc.dram_tensor("w_out", [DI, D], F32, kind="ExternalInput"))
    w1 = ap(nc.dram_tensor("w1", [FF, D], F32, kind="ExternalInput"))
    w1t = ap(nc.dram_tensor("w1t", [D, FF], F32, kind="ExternalInput"))
    b1 = ap(nc.dram_tensor("b1", [128, NF], F32, kind="ExternalInput"))
    w2 = ap(nc.dram_tensor("w2", [D, FF], F32, kind="ExternalInput"))
    w2t = ap(nc.dram_tensor("w2t", [FF, D], F32, kind="ExternalInput"))
    b2 = ap(nc.dram_tensor("b2", [128, ND], F32, kind="ExternalInput"))
    ident = ap(nc.dram_tensor("ident", [128, 128], F32, kind="ExternalInput"))
    out = ap(nc.dram_tensor("out", [T, D], F32, kind="ExternalOutput"))

    x1_dram = ap(nc.dram_tensor("x1_dram", [T, D], F32))
    x1q_dram = ap(nc.dram_tensor("x1q_dram", [D, T], BF16))
    h_dram = ap(nc.dram_tensor("h_dram", [FF, T], F32))
    xq2_dram = ap(nc.dram_tensor("xq2_dram", [FF, T], BF16))
    wob_dram = ap(nc.dram_tensor("wob_dram", [DI, D], BF16))
    bc_dram = ap(nc.dram_tensor("bc_dram", [NTILE, 32, TT], F32))
    vec_dram = ap(nc.dram_tensor("vec_dram", [8, 4096], F32))
    cc1_in = ap(nc.dram_tensor("cc1_in", [1, 128], F32))
    cc1_out = ap(nc.dram_tensor("cc1_out", [1, 128], F32, addr_space="Shared"))
    cc2_in = ap(nc.dram_tensor("cc2_in", [1, 128], F32))
    cc2_out = ap(nc.dram_tensor("cc2_out", [1, 128], F32, addr_space="Shared"))

    dbg_aps = {}
    for name, shape in dbg:
        dbg_aps[name] = ap(nc.dram_tensor(name, shape, F32, kind="ExternalOutput"))

    import contextlib
    es = contextlib.ExitStack()
    with es:
        arena = es.enter_context(nc.sbuf_tensor("arena", [128, ARENA_W], F32))
        winb = es.enter_context(nc.sbuf_tensor("winb", [128, ND * 2 * DI], BF16))
        wxpb = es.enter_context(nc.sbuf_tensor("wxpb", [128, NCT, 96], BF16))
        wdtb = es.enter_context(nc.sbuf_tensor("wdtb", [64, NCT, 128], BF16))
        wcv = es.enter_context(nc.sbuf_tensor("wcv", [128, NCT, 4], F32))
        bcv = es.enter_context(nc.sbuf_tensor("bcv", [128, NCT], F32))
        bdt = es.enter_context(nc.sbuf_tensor("bdt", [128, NCT], F32))
        amat = es.enter_context(nc.sbuf_tensor("amat", [128, NCT, N], F32))
        dmat = es.enter_context(nc.sbuf_tensor("dmat", [128, NCT], F32))
        identb = es.enter_context(nc.sbuf_tensor("identb", [128, 128], F32))
        xfb = es.enter_context(nc.sbuf_tensor("xfb", [128, ND, TT], BF16))
        xfstg = es.enter_context(nc.sbuf_tensor("xfstg", [128, ND * TT], F32))
        sgz = es.enter_context(nc.sbuf_tensor("sgz", [128, NCT, TT], BF16))
        xcsb = es.enter_context(nc.sbuf_tensor("xcsb", [128, NCT, TT], BF16))
        dbc = es.enter_context(nc.sbuf_tensor("dbc", [96, TT], F32))
        dbcb = es.enter_context(nc.sbuf_tensor("dbcb", [64, TT], BF16))
        gtb = es.enter_context(nc.sbuf_tensor("gtb", [128, NCT * TT], BF16))
        wob = es.enter_context(nc.sbuf_tensor("wob", [128, 2, D], BF16))
        xtk = es.enter_context(nc.sbuf_tensor("xtk", [128, D], F32))
        lnv = es.enter_context(nc.sbuf_tensor("lnv", [128, D], F32))
        cstat = es.enter_context(nc.sbuf_tensor("cstat", [128, 8], F32))
        padv = es.enter_context(nc.sbuf_tensor("padv", [128, 2], F32))
        xmax = es.enter_context(nc.sbuf_tensor("xmax", [128, 1], F32))
        hmax = es.enter_context(nc.sbuf_tensor("hmax", [128, 1], F32))
        sml = es.enter_context(nc.sbuf_tensor("sml", [1, 128], F32))
        smlb = es.enter_context(nc.sbuf_tensor("smlb", [1, 128], F32))
        scal = es.enter_context(nc.sbuf_tensor("scal", [128, 16], F32))
        sw1 = es.enter_context(nc.sbuf_tensor("sw1", [128, NF], F32))
        sw2 = es.enter_context(nc.sbuf_tensor("sw2", [128, ND], F32))
        bq1 = es.enter_context(nc.sbuf_tensor("bq1", [128, NF], F32))
        sprod1 = es.enter_context(nc.sbuf_tensor("sprod1", [128, NF], F32))
        x1qtb = es.enter_context(nc.sbuf_tensor("x1qtb", [128, 128], BF16))
        xq2c = es.enter_context(nc.sbuf_tensor("xq2c", [128, 2, 128], BF16))
        pa0 = es.enter_context(nc.psum_tensor("pa0", [128, 512], F32))
        pa1 = es.enter_context(nc.psum_tensor("pa1", [128, 512], F32))
        pa2 = es.enter_context(nc.psum_tensor("pa2", [128, 512], F32))
        pa3 = es.enter_context(nc.psum_tensor("pa3", [128, 512], F32))
        pb0 = es.enter_context(nc.psum_tensor("pb0", [128, 512], F32))
        pb1 = es.enter_context(nc.psum_tensor("pb1", [128, 512], F32))
        pb2 = es.enter_context(nc.psum_tensor("pb2", [128, 512], F32))
        pb3 = es.enter_context(nc.psum_tensor("pb3", [128, 512], F32))
        semnames = ["te", "ve", "se", "gp",
                    "c", "stg", "xf", "xt", "wo", "qc", "rhs", "bc",
                    "spb", "spx", "sph", "spw", "spq", "vb", "st", "wo2", "sph2"]
        semobjs = {nm: es.enter_context(nc.semaphore("s_" + nm)) for nm in semnames}
        blk = es.enter_context(nc.Block())
        sems = semobjs
        s_te, s_ve, s_se, s_gp = sems["te"], sems["ve"], sems["se"], sems["gp"]
        P = Plan()
        phase_done = False

        def asl(off, *shape):
            n = int(np.prod(shape))
            a = arena[:, off:off + n]
            if len(shape) > 1:
                names = " ".join(f"d{i}" for i in range(len(shape)))
                a = a.rearrange(f"p ({names}) -> p {names}",
                                **{f"d{i}": s for i, s in enumerate(shape)})
            return a, off + n

        o = 0
        xin, o = asl(o, NCT, TT + 4)
        xcs, o = asl(o, NCT, TT)
        dtv, o = asl(o, NCT, TT)
        xdt, o = asl(o, NCT, TT)
        yt, o = asl(o, NCT, TT)
        da, o = asl(o, TC, NCT, N)
        uu, o = asl(o, TC, NCT, N)
        shA, o = asl(o, TC, NCT, N)
        shB, o = asl(o, TC, NCT, N)
        ypr, o = asl(o, TC, NCT, N)
        st1, o = asl(o, NCT * N)
        brep, o = asl(o, N, TT)
        crep, o = asl(o, N, TT)
        assert o <= ARENA_W, o

        o = 0
        qV, o = asl(o, 4096)
        qK, o = asl(o, 4096)
        qR, o = asl(o, 4096)
        qC, o = asl(o, 4096)
        qRep, o = asl(o, 4096)
        lnq2, o = asl(o, 1024)
        repv, o = asl(o, 1024)
        repv2, o = asl(o, 1024)
        assert o <= ARENA_W
        stg = qV
        lnq = xdt[:, 0:8].rearrange("p c t -> p (c t)")

        def E(eng, fn):
            P.emit(eng, fn)

        def W(eng, name, val):
            if val:
                E(eng, lambda h, s=sems[name], v=val: h.wait_ge(s, v))

        def INC(name, by=1):
            P.cnt[name] = P.cnt.get(name, 0) + by
            return P.cnt[name]

        def dma(eng, dst, src, semname):
            tot = INC(semname, 16)
            E(eng, lambda h, d=dst, sr=src, s=sems[semname]:
              h.dma_start(out=d, in_=sr).then_inc(s, 16))
            return tot

        def bump(eng, semname):
            tot = INC(semname, 1)
            old = P.q[eng].pop()
            P.emit(eng, lambda h, old=old, s=sems[semname]: old(h).then_inc(s, 1))
            return tot

        def te_group(mms, semname="te"):
            tot = INC(semname, 1)
            for k, fn in enumerate(mms):
                if k == len(mms) - 1:
                    E("tensor", lambda h, fn=fn, s=sems[semname]: fn(h).then_inc(s, 1))
                else:
                    E("tensor", fn)
            return tot

        def flat(apx):
            shp = apx.shape
            if len(shp) == 2:
                return apx
            names = " ".join(f"d{i}" for i in range(len(shp) - 1))
            return apx.rearrange(f"p {names} -> p ({names})")

        def vpad():
            E("vector", lambda h: h.tensor_scalar(
                padv[:, 1:2], padv[:, 0:1], 1.0, None, ALU.mult))

        def recip_nr(dst, src, w):
            vpad()
            tv = bump("vector", "ve")
            W("scalar", "ve", tv)
            E("scalar", lambda h: h.activation(w, src, ACT.Ln))
            E("scalar", lambda h: h.activation(dst, w, ACT.Exp, scale=-1.0))
            ts = bump("scalar", "se")
            W("vector", "se", ts)
            E("vector", lambda h: h.tensor_tensor(w, dst, src, ALU.mult))
            vpad()
            E("vector", lambda h: h.tensor_scalar(w, w, -1.0, 2.0, ALU.mult, ALU.add))
            vpad()
            E("vector", lambda h: h.tensor_tensor(dst, dst, w, ALU.mult))

        def rsqrt_nr(dst, src, w):
            vpad()
            tv = bump("vector", "ve")
            W("scalar", "ve", tv)
            E("scalar", lambda h: h.activation(w, src, ACT.Ln))
            E("scalar", lambda h: h.activation(dst, w, ACT.Exp, scale=-0.5))
            ts = bump("scalar", "se")
            W("vector", "se", ts)
            E("vector", lambda h: h.tensor_tensor(w, dst, dst, ALU.mult))
            vpad()
            E("vector", lambda h: h.tensor_tensor(w, w, src, ALU.mult))
            vpad()
            E("vector", lambda h: h.tensor_scalar(w, w, -0.5, 1.5, ALU.mult, ALU.add))
            vpad()
            E("vector", lambda h: h.tensor_tensor(dst, dst, w, ALU.mult))

        def round_tail(v, k0, r, c):
            E("vector", lambda h: h.tensor_tensor(r, v, k0, ALU.subtract))
            E("vector", lambda h: h.tensor_scalar(c, r, 0.5, None, ALU.is_gt))
            E("vector", lambda h: h.tensor_scalar(r, r, -0.5, None, ALU.is_lt))
            E("vector", lambda h: h.tensor_tensor(c, c, r, ALU.subtract))
            E("vector", lambda h: h.tensor_tensor(v, k0, c, ALU.add))

        def round_chain(v, k0, r, c, pads=False):
            p = vpad if pads else (lambda: None)
            p()
            E("vector", lambda h: h.tensor_scalar(k0, v, MAGIC, MAGIC, ALU.add, ALU.subtract))
            p()
            E("vector", lambda h: h.tensor_tensor(r, v, k0, ALU.subtract))
            p()
            E("vector", lambda h: h.tensor_scalar(c, r, 0.5, None, ALU.is_gt))
            E("vector", lambda h: h.tensor_scalar(r, r, -0.5, None, ALU.is_lt))
            p()
            E("vector", lambda h: h.tensor_tensor(c, c, r, ALU.subtract))
            p()
            E("vector", lambda h: h.tensor_tensor(v, k0, c, ALU.add))

        def _early_out():
            E("vector", lambda h: h.memset(lnv[:], 0.0))
            tv_ = bump("vector", "ve")
            W("gpsimd", "ve", tv_)
            for tb_ in range(NTILE):
                dma("gpsimd", out[tb_ * TT:(tb_ + 1) * TT, :], lnv[:], "st")
            if "dbg_x1" in dbg_aps:
                W("gpsimd", "spx", P.cnt.get("spx", 0))
                dma("gpsimd", dbg_aps["dbg_x1"], x1_dram, "st")
            W("gpsimd", "st", P.cnt["st"])

        for dst, src in [
            (wcv[:], w_conv.rearrange("p (c k) -> p c k", c=NCT)),
            (bcv[:], b_conv), (bdt[:], b_dt),
            (amat[:], a_log.rearrange("p (c n) -> p c n", c=NCT)),
            (dmat[:], d_vec), (identb[:], ident),
        ]:
            t_consts = dma("sync", dst, src, "c")
        W("scalar", "c", t_consts)
        E("scalar", lambda h: h.activation(flat(amat[:]), flat(amat[:]), ACT.Exp))
        t_se = bump("scalar", "se")
        W("vector", "se", t_se)
        E("vector", lambda h: h.tensor_scalar(flat(amat[:]), flat(amat[:]), -1.0, None, ALU.mult))
        bump("vector", "ve")
        if phase == "init2":
            _early_out()
            phase_done = True

        winb3 = winb.rearrange("p (c w) -> p c w", c=ND)
        if phase == "init2":
            ndo = 0
        else:
            ndo = ND
        for kc in range(ndo):
            t = dma("sync", stg, w_in[128 * kc:128 * (kc + 1), :], "stg")
            W("vector", "stg", t)
            E("vector", lambda h, kc=kc: h.tensor_copy(winb3[:, kc], stg))
            tv = bump("vector", "ve")
            W("sync", "ve", tv)
        if phase == "init3":
            _early_out()
            phase_done = True
        for kc in range(0 if phase in ("init2", "init3") else NCT):
            t = dma("sync", stg[:, 0:96], w_xp[128 * kc:128 * (kc + 1), :], "stg")
            W("vector", "stg", t)
            E("vector", lambda h, kc=kc: h.tensor_copy(wxpb[:, kc], stg[:, 0:96]))
            tv = bump("vector", "ve")
            W("sync", "ve", tv)
        if phase == "init4":
            _early_out()
            phase_done = True
        if phase in ("init2", "init3", "init4"):
            skip_wdt = True
        else:
            skip_wdt = False
        t = None if skip_wdt else dma("sync", stg[0:64, 0:DI], w_dt, "stg")
        if not skip_wdt:
            W("vector", "stg", t)
            E("vector", lambda h: h.tensor_copy(
                wdtb[:].rearrange("p c w -> p (c w)"), stg[0:64, 0:DI]))
            tv = bump("vector", "ve")
            W("sync", "ve", tv)
        if phase == "init5":
            _early_out()
            phase_done = True
        for c in range(0 if phase in ("init2", "init3", "init4", "init5") else NCT):
            t = dma("sync", stg[:, 0:D], w_out[128 * c:128 * (c + 1), :], "stg")
            W("vector", "stg", t)
            E("vector", lambda h: h.tensor_copy(wob[:, 0], stg[:, 0:D]))
            tv = bump("vector", "ve")
            W("gpsimd", "ve", tv)
            tsp = dma("gpsimd", wob_dram[128 * c:128 * (c + 1), :], wob[:, 0], "spw")
            W("sync", "spw", tsp)

        E("vector", lambda h: h.memset(padv[:], 0.0))
        E("vector", lambda h: h.memset(xin[:, :, 0:4], 0.0))
        E("vector", lambda h: h.memset(flat(shA[:]), 0.0))
        E("vector", lambda h: h.memset(flat(shB[:]), 0.0))
        E("vector", lambda h: h.memset(smlb[:], NEG_BIG))
        E("vector", lambda h: h.memset(xmax[:], NEG_BIG))
        E("vector", lambda h: h.memset(hmax[:], NEG_BIG))
        bump("vector", "ve")

        if phase == "init" and not phase_done:
            _early_out()
            phase_done = True
        t_dtz_prev = 0
        t_ln_prev = 0
        t_spill_prev = 0
        t_xfb_prev = 0
        t_ctr_prev = 0
        t_evA_prev = 0
        t_xp_prev = 0
        t_wo_used_prev = [0, 0]
        t_waveB_prev = 0
        ntile_run = 0 if phase_done else (1 if phase == "tile0" else NTILE)
        pro = {}

        def emit_proA(i, t_g_prevtile):
            nonlocal t_xfb_prev, t_evA_prev, t_waveB_prev
            t0_ = i * TT
            W("sync", "ve", t_xfb_prev)
            tfx = dma("sync", xfstg[:].rearrange("p (c t) -> p c t", c=ND),
                      x_feat.rearrange("(c p) t -> p c t", p=128)[:, :, t0_:t0_ + TT], "xf")
            W("vector", "xf", tfx)
            W("vector", "te", t_waveB_prev)
            E("vector", lambda h: h.tensor_copy(flat(xfb[:]), xfstg[:]))
            t_xfb = bump("vector", "ve")
            t_xfb_prev = t_xfb
            if i > 0:
                W("vector", "se", t_evA_prev)
                E("vector", lambda h: h.tensor_copy(xin[:, :, 0:3], xin[:, :, TT:TT + 3]))
                t_pad = bump("vector", "ve")
            else:
                t_pad = t_xfb
            W("tensor", "ve", t_pad)
            W("tensor", "se", t_dtz_prev)
            mms = []
            for oc in range(16):
                bank = (pa0, pa1, pa2, pa3)[oc % 4]
                slot = (oc // 4) * TT
                for kc in range(ND):
                    mms.append(lambda h, oc=oc, kc=kc, bank=bank, slot=slot:
                               h.matmul(bank[:, slot:slot + TT],
                                        winb3[:, kc, 128 * oc:128 * (oc + 1)],
                                        xfb[:, kc], start=(kc == 0), stop=(kc == ND - 1)))
            t_teA = te_group(mms)
            W("scalar", "te", t_teA)
            for oc in range(16):
                bank = (pa0, pa1, pa2, pa3)[oc % 4]
                slot = (oc // 4) * TT
                E("scalar", lambda h, oc=oc, bank=bank, slot=slot:
                  h.activation(xin[:, oc, 3:3 + TT], bank[:, slot:slot + TT],
                               ACT.Copy))
            t_evA = bump("scalar", "se")
            t_evA_prev = t_evA
            W("scalar", "ve", t_g_prevtile)
            for c in range(NCT):
                E("scalar", lambda h, c=c: h.activation(
                    xcs[:, c], xin[:, c, 0:TT], ACT.Identity, scale=wcv[:, c, 0:1]))
            t_tap = bump("scalar", "se")
            W("vector", "se", t_tap)
            for c in range(NCT):
                for k in range(1, 4):
                    E("vector", lambda h, c=c, k=k: h.scalar_tensor_tensor(
                        xcs[:, c], xin[:, c, k:k + TT], wcv[:, c, k:k + 1], xcs[:, c],
                        ALU.mult, ALU.add))
            t_conv = bump("vector", "ve")
            pro[i] = (t_evA, t_conv)

        if ntile_run > 0:
            emit_proA(0, 0)
        for i in range(ntile_run):
            t0 = i * TT
            t_evA, t_conv = pro.pop(i)
            W("sync", "ve", t_ctr_prev)
            ttk = dma("sync", xtk[:], x_tok[t0:t0 + TT, :], "xt")

            W("tensor", "se", t_evA)
            mms = []
            for oc in range(16):
                och = 16 + oc
                bank = (pa0, pa1, pa2, pa3)[oc % 4]
                slot = (oc // 4) * TT
                for kc in range(ND):
                    mms.append(lambda h, och=och, kc=kc, bank=bank, slot=slot:
                               h.matmul(bank[:, slot:slot + TT],
                                        winb3[:, kc, 128 * och:128 * (och + 1)],
                                        xfb[:, kc], start=(kc == 0), stop=(kc == ND - 1)))
            t_te = te_group(mms)
            W("scalar", "te", t_te)
            for oc in range(16):
                bank = (pa0, pa1, pa2, pa3)[oc % 4]
                slot = (oc // 4) * TT
                E("scalar", lambda h, oc=oc, bank=bank, slot=slot:
                  h.activation(sgz[:, oc], bank[:, slot:slot + TT], ACT.Silu))
            t_evB = bump("scalar", "se")
            t_waveB_prev = t_te
            W("scalar", "ve", t_conv)
            for c in range(NCT):
                E("scalar", lambda h, c=c: h.activation(
                    xcs[:, c], xcs[:, c], ACT.Silu, bias=bcv[:, c:c + 1]))
            t_xcs = bump("scalar", "se")

            W("scalar", "te", t_xp_prev)
            E("scalar", lambda h: h.activation(flat(xcsb[:]), flat(xcs[:]), ACT.Copy))
            t_xcsb = bump("scalar", "se")
            W("tensor", "se", t_xcsb)
            t_xp = te_group([lambda h, c=c: h.matmul(
                    pb2[0:96, 0:TT], wxpb[:, c], xcsb[:, c],
                    start=(c == 0), stop=(c == NCT - 1)) for c in range(NCT)])
            t_xp_prev = t_xp
            W("scalar", "te", t_xp)
            E("scalar", lambda h: h.activation(dbc[:], pb2[0:96, 0:TT], ACT.Copy))
            E("scalar", lambda h: h.activation(dbcb[:], dbc[0:64, :], ACT.Copy))
            t_dbc = bump("scalar", "se")

            W("gpsimd", "se", t_dbc)
            tsp = dma("gpsimd", bc_dram[i], dbc[64:96, :], "spb")
            W("gpsimd", "spb", tsp)
            tg = dma("gpsimd", brep[:, 0:8], bc_dram[i, 0:8, :].partition_broadcast(128), "bc")
            tg = dma("gpsimd", brep[:, 8:16], bc_dram[i, 8:16, :].partition_broadcast(128), "bc")
            tg = dma("gpsimd", crep[:, 0:8], bc_dram[i, 16:24, :].partition_broadcast(128), "bc")
            tg = dma("gpsimd", crep[:, 8:16], bc_dram[i, 24:32, :].partition_broadcast(128), "bc")
            t_rep = tg

            W("tensor", "se", t_dbc)
            W("tensor", "se", t_evB)
            t_dtp = te_group([lambda h, c=c, bank=(pa0, pa1, pa2, pa3)[c % 4],
                              slot=(c // 4) * TT: h.matmul(
                    bank[:, slot:slot + TT], wdtb[:, c], dbcb[:], start=True, stop=True)
                    for c in range(NCT)])
            W("scalar", "te", t_dtp)
            for c in range(NCT):
                bank = (pa0, pa1, pa2, pa3)[c % 4]
                slot = (c // 4) * TT
                E("scalar", lambda h, c=c, bank=bank, slot=slot: h.activation(
                    dtv[:, c], bank[:, slot:slot + TT],
                    ACT.Identity, bias=bdt[:, c:c + 1]))
            t_dtz = bump("scalar", "se")
            t_dtz_prev = t_dtz
            E("scalar", lambda h: h.activation(flat(xdt[:]), flat(dtv[:]), ACT.Abs))
            E("scalar", lambda h: h.activation(flat(xdt[:]), flat(xdt[:]), ACT.Exp, scale=-1.0))
            E("scalar", lambda h: h.activation(flat(xdt[:]), flat(xdt[:]), ACT.Ln, bias=1.0))
            E("scalar", lambda h: h.activation(flat(dtv[:]), flat(dtv[:]), ACT.Relu))
            t_spl = bump("scalar", "se")
            W("vector", "se", t_spl)
            E("vector", lambda h: h.tensor_tensor(flat(dtv[:]), flat(dtv[:]), flat(xdt[:]), ALU.add))
            E("vector", lambda h: h.tensor_tensor(flat(xdt[:]), flat(xcs[:]), flat(dtv[:]), ALU.mult))
            bump("vector", "ve")

            W("vector", "bc", t_rep)
            for sk in range(NSUB):
                ts0 = sk * TC
                bank = shA if sk % 2 == 0 else shB
                oban = shB if sk % 2 == 0 else shA
                E("vector", lambda h, ts0=ts0: h.tensor_tensor(
                    da[:],
                    dtv[:, :, ts0:ts0 + TC].rearrange("p c t -> p t c")[:, :, :, None]
                    .broadcast_to([128, TC, NCT, N]),
                    amat[:][:, None].broadcast_to([128, TC, NCT, N]),
                    ALU.mult))
                t_m = bump("vector", "ve")
                E("vector", lambda h, ts0=ts0: h.tensor_tensor(
                    uu[:],
                    xdt[:, :, ts0:ts0 + TC].rearrange("p c t -> p t c")[:, :, :, None]
                    .broadcast_to([128, TC, NCT, N]),
                    brep[:, :, ts0:ts0 + TC].rearrange("p n t -> p t n")[:, :, None, :]
                    .broadcast_to([128, TC, NCT, N]),
                    ALU.mult))
                W("scalar", "ve", t_m)
                E("scalar", lambda h: h.activation(flat(da[:]), flat(da[:]), ACT.Exp))
                t_exp = bump("scalar", "se")
                W("vector", "se", t_exp)
                for j in range(TC):
                    prev = oban[:, TC - 1] if j == 0 else bank[:, j - 1]
                    E("vector", lambda h, j=j, prev=prev: h.scalar_tensor_tensor(
                        st1, flat(da[:, j]), EM10, flat(prev), ALU.max, ALU.mult))
                    E("vector", lambda h, j=j, bank=bank: h.tensor_tensor(
                        flat(bank[:, j]), st1, flat(uu[:, j]), ALU.add))
                    E("vector", lambda h, j=j, bank=bank: h.tensor_scalar(
                        flat(bank[:, j]), flat(bank[:, j]), 10.0, -10.0, ALU.min, ALU.max))
                t_ser = bump("vector", "ve")
                E("vector", lambda h, ts0=ts0, bank=bank: h.tensor_tensor(
                    ypr[:], bank[:],
                    crep[:, :, ts0:ts0 + TC].rearrange("p n t -> p t n")[:, :, None, :]
                    .broadcast_to([128, TC, NCT, N]),
                    ALU.mult))
                E("vector", lambda h, ts0=ts0: h.tensor_reduce(
                    yt[:, :, ts0:ts0 + TC].rearrange("p c t -> p t c"),
                    ypr[:], AX.X, ALU.add))
                bump("vector", "ve")

            E("vector", lambda h: h.tensor_copy(flat(uu[:]), flat(sgz[:])))
            for c in range(NCT):
                E("vector", lambda h, c=c: h.scalar_tensor_tensor(
                    yt[:, c], xcs[:, c], dmat[:, c:c + 1], yt[:, c], ALU.mult, ALU.add))
            E("vector", lambda h: h.tensor_tensor(flat(uu[:]), flat(uu[:]), flat(yt[:]), ALU.mult))
            E("vector", lambda h: h.tensor_copy(gtb[:], flat(uu[:])))
            t_g = bump("vector", "ve")

            if i + 1 < ntile_run:
                emit_proA(i + 1, t_g)

            gtb3 = gtb.rearrange("p (c t) -> p c t", c=NCT)
            W("tensor", "ve", t_g)
            W("tensor", "ve", t_ln_prev)
            slotB = flat(sgz[:]).rearrange("p (c d) -> p c d", c=2)
            slots = [wob[:], slotB]
            NWG = 8
            t_wo_used = [t_wo_used_prev[0], t_wo_used_prev[1]]
            tld_wo = {}

            def issue_wo(g):
                si = g % 2
                stream = "wo" if si == 0 else "wo2"
                W("sync", "te", t_wo_used[si])
                if si == 1:
                    W("sync", "ve", t_g)
                tld_wo[g] = dma(
                    "sync", slots[si],
                    wob_dram[256 * g:256 * (g + 1), :]
                    .rearrange("(c p) d -> p c d", p=128), stream)

            issue_wo(0)
            issue_wo(1)
            for g in range(NWG):
                si = g % 2
                stream = "wo" if si == 0 else "wo2"
                W("tensor", stream, tld_wo[g])
                mms = []
                for cc in range(2):
                    c = 2 * g + cc
                    mms += [
                        lambda h, c=c, si=si, cc=cc: h.matmul(
                            pb0[:, 0:512], gtb3[:, c], slots[si][:, cc, 0:512],
                            start=(c == 0), stop=(c == NCT - 1)),
                        lambda h, c=c, si=si, cc=cc: h.matmul(
                            pb1[:, 0:512], gtb3[:, c], slots[si][:, cc, 512:1024],
                            start=(c == 0), stop=(c == NCT - 1))]
                t_wo = te_group(mms)
                t_wo_used[si] = t_wo
                if g + 2 < NWG:
                    issue_wo(g + 2)
            t_wo_used_prev = list(t_wo_used)

            W("vector", "te", t_wo)
            W("vector", "spx", t_spill_prev)
            W("vector", "xt", ttk)
            E("vector", lambda h: h.tensor_tensor(lnv[:, 0:512], pb0[:], xtk[:, 0:512], ALU.add))
            E("vector", lambda h: h.tensor_tensor(lnv[:, 512:1024], pb1[:], xtk[:, 512:1024], ALU.add))
            if i == 0 and "dbg_lnp" in dbg_aps:
                E("vector", lambda h: h.tensor_copy(qK[:, 0:1024], lnv[:]))
            E("vector", lambda h: h.tensor_reduce(cstat[:, 0:1], lnv[:], AX.X, ALU.add))
            vpad()
            E("vector", lambda h: h.tensor_scalar(cstat[:, 1:2], cstat[:, 0:1], 1.0 / D, None, ALU.mult))
            vpad()
            E("vector", lambda h: h.tensor_scalar(lnv[:], lnv[:], cstat[:, 1:2], None, ALU.subtract))
            t_ctr = bump("vector", "ve")
            t_ln_prev = t_ctr
            t_ctr_prev = t_ctr
            W("scalar", "ve", t_ctr)
            E("scalar", lambda h: h.activation(lnq, lnv[:], ACT.Square, accum_out=cstat[:, 2:3]))
            t_sq = bump("scalar", "se")
            W("vector", "se", t_sq)
            E("vector", lambda h: h.tensor_scalar(cstat[:, 3:4], cstat[:, 2:3], 1.0 / D, LN_EPS, ALU.mult, ALU.add))
            rsqrt_nr(cstat[:, 5:6], cstat[:, 3:4], cstat[:, 4:5])
            vpad()
            E("vector", lambda h: h.tensor_scalar(lnv[:], lnv[:], cstat[:, 5:6], None, ALU.mult))
            E("vector", lambda h: h.tensor_reduce(cstat[:, 6:7], lnv[:], AX.X, ALU.max))
            vpad()
            E("vector", lambda h: h.tensor_tensor(xmax[:], xmax[:], cstat[:, 6:7], ALU.max))
            t_x1 = bump("vector", "ve")
            W("gpsimd", "ve", t_x1)
            t_spill_prev = dma("gpsimd", x1_dram[t0:t0 + TT, :], lnv[:], "spx")

        mamba_ve = P.cnt.get("ve", 0)
        if not phase_done and phase in ("tile0", "mamba"):
            W("gpsimd", "ve", mamba_ve)
            tst = dma("gpsimd", out[0:TT, :], lnv[:], "st")
            if "dbg_x1" in dbg_aps:
                W("gpsimd", "spx", P.cnt.get("spx", 0))
                dma("gpsimd", dbg_aps["dbg_x1"], x1_dram, "st")
            for nm, src_ap in [("dbg_xin", flat(xin[:])), ("dbg_xcs", flat(xcs[:])),
                               ("dbg_dtv", flat(dtv[:])), ("dbg_yt", flat(yt[:])),
                               ("dbg_dbc", dbc[:]), ("dbg_xdt", flat(xdt[:])),
                               ("dbg_g", flat(uu[:])), ("dbg_lnp", qK[:, 0:1024]),
                               ("dbg_xtk", xtk[:]), ("dbg_cst", cstat[:, 0:8]),
                               ("dbg_lnq", lnq)]:
                if nm in dbg_aps:
                    dma("gpsimd", dbg_aps[nm], src_ap, "st")
            W("gpsimd", "st", P.cnt["st"])
            phase_done = True

        if not phase_done:
            if not phase_done:

                phase_done = True
            else:
                phase_done = False
            t_dtz_prev = 0
            t_ln_prev = 0
            t_spill_prev = 0
            t_xfb_prev = 0
            t_ctr_prev = 0
            ntile_run = 0 if phase_done else (1 if phase == "tile0" else NTILE)
            for i in range(ntile_run):
                t0 = i * TT
                W("sync", "ve", t_xfb_prev)
                tfx = dma("sync", xfstg[:].rearrange("p (c t) -> p c t", c=ND),
                          x_feat.rearrange("(c p) t -> p c t", p=128)[:, :, t0:t0 + TT], "xf")
                W("sync", "ve", t_ctr_prev)
                ttk = dma("sync", xtk[:], x_tok[t0:t0 + TT, :], "xt")
                W("vector", "xf", tfx)
                E("vector", lambda h: h.tensor_copy(flat(xfb[:]), xfstg[:]))
                t_xfb = bump("vector", "ve")
                t_xfb_prev = t_xfb
                if i > 0:
                    E("vector", lambda h: h.tensor_copy(xin[:, :, 0:3], xin[:, :, TT:TT + 3]))
                t_pad = bump("vector", "ve")

                t_evA = t_evB = 0
                for wave in range(2):
                    if wave == 0:
                        W("tensor", "ve", t_pad)
                        W("tensor", "se", t_dtz_prev)
                    else:
                        W("tensor", "ve", t_evA)
                    mms = []
                    for oc in range(16):
                        och = wave * 16 + oc
                        bank = (pa0, pa1, pa2, pa3)[oc % 4]
                        slot = (oc // 4) * TT
                        for kc in range(ND):
                            mms.append(lambda h, och=och, kc=kc, bank=bank, slot=slot:
                                       h.matmul(bank[:, slot:slot + TT],
                                                winb3[:, kc, 128 * och:128 * (och + 1)],
                                                xfb[:, kc], start=(kc == 0), stop=(kc == ND - 1)))
                    t_te = te_group(mms)
                    if wave == 0:
                        W("vector", "te", t_te)
                        for oc in range(16):
                            bank = (pa0, pa1, pa2, pa3)[oc % 4]
                            slot = (oc // 4) * TT
                            E("vector", lambda h, oc=oc, bank=bank, slot=slot:
                              h.tensor_copy(xin[:, oc, 3:3 + TT], bank[:, slot:slot + TT]))
                        t_evA = bump("vector", "ve")
                    else:
                        W("scalar", "te", t_te)
                        for oc in range(16):
                            bank = (pa0, pa1, pa2, pa3)[oc % 4]
                            slot = (oc // 4) * TT
                            E("scalar", lambda h, oc=oc, bank=bank, slot=slot:
                              h.activation(sgz[:, oc], bank[:, slot:slot + TT], ACT.Silu))
                        t_evB = bump("scalar", "se")

                W("vector", "ve", t_evA)
                for c in range(NCT):
                    E("vector", lambda h, c=c: h.tensor_scalar(
                        xcs[:, c], xin[:, c, 0:TT], wcv[:, c, 0:1], None, ALU.mult))
                    for k in range(1, 4):
                        E("vector", lambda h, c=c, k=k: h.scalar_tensor_tensor(
                            xcs[:, c], xin[:, c, k:k + TT], wcv[:, c, k:k + 1], xcs[:, c],
                            ALU.mult, ALU.add))
                t_conv = bump("vector", "ve")
                W("scalar", "ve", t_conv)
                for c in range(NCT):
                    E("scalar", lambda h, c=c: h.activation(
                        xcs[:, c], xcs[:, c], ACT.Silu, bias=bcv[:, c:c + 1]))
                t_xcs = bump("scalar", "se")

                W("vector", "se", t_xcs)
                E("vector", lambda h: h.tensor_copy(flat(xcsb[:]), flat(xcs[:])))
                t_xcsb = bump("vector", "ve")
                W("tensor", "ve", t_xcsb)
                t_xp = te_group([lambda h, c=c: h.matmul(
                        pb2[0:96, 0:TT], wxpb[:, c], xcsb[:, c],
                        start=(c == 0), stop=(c == NCT - 1)) for c in range(NCT)])
                W("vector", "te", t_xp)
                E("vector", lambda h: h.tensor_copy(dbc[:], pb2[0:96, 0:TT]))
                E("vector", lambda h: h.tensor_copy(dbcb[:], dbc[0:64, :]))
                t_dbc = bump("vector", "ve")

                W("gpsimd", "ve", t_dbc)
                tsp = dma("gpsimd", bc_dram[i], dbc[64:96, :], "spb")
                W("gpsimd", "spb", tsp)
                tg = dma("gpsimd", brep[:, 0:8], bc_dram[i, 0:8, :].partition_broadcast(128), "bc")
                tg = dma("gpsimd", brep[:, 8:16], bc_dram[i, 8:16, :].partition_broadcast(128), "bc")
                tg = dma("gpsimd", crep[:, 0:8], bc_dram[i, 16:24, :].partition_broadcast(128), "bc")
                tg = dma("gpsimd", crep[:, 8:16], bc_dram[i, 24:32, :].partition_broadcast(128), "bc")
                t_rep = tg

                W("tensor", "ve", t_dbc)
                W("tensor", "se", t_evB)
                t_dtp = te_group([lambda h, c=c, bank=(pa0, pa1, pa2, pa3)[c % 4],
                                  slot=(c // 4) * TT: h.matmul(
                        bank[:, slot:slot + TT], wdtb[:, c], dbcb[:], start=True, stop=True)
                        for c in range(NCT)])
                W("scalar", "te", t_dtp)
                for c in range(NCT):
                    bank = (pa0, pa1, pa2, pa3)[c % 4]
                    slot = (c // 4) * TT
                    E("scalar", lambda h, c=c, bank=bank, slot=slot: h.activation(
                        spA[:, TT * c:TT * (c + 1)], bank[:, slot:slot + TT],
                        ACT.Identity, bias=bdt[:, c:c + 1]))
                t_dtz = bump("scalar", "se")
                t_dtz_prev = t_dtz
                E("scalar", lambda h: h.activation(spB, spA, ACT.Abs))
                E("scalar", lambda h: h.activation(spB, spB, ACT.Exp, scale=-1.0))
                E("scalar", lambda h: h.activation(spB, spB, ACT.Ln, bias=1.0))
                t_spl = bump("scalar", "se")
                W("vector", "se", t_spl)
                E("vector", lambda h: h.tensor_scalar(flat(dtv[:]), spA, 0.0, None, ALU.max))
                E("vector", lambda h: h.tensor_tensor(flat(dtv[:]), flat(dtv[:]), spB, ALU.add))
                E("vector", lambda h: h.tensor_tensor(flat(xdt[:]), flat(xcs[:]), flat(dtv[:]), ALU.mult))
                bump("vector", "ve")

                W("vector", "bc", t_rep)
                for sk in range(NSUB):
                    ts0 = sk * TC
                    E("vector", lambda h, ts0=ts0: h.tensor_tensor(
                        da[:],
                        dtv[:, :, ts0:ts0 + TC].rearrange("p c t -> p t c")[:, :, :, None]
                        .broadcast_to([128, TC, NCT, N]),
                        amat[:][:, None].broadcast_to([128, TC, NCT, N]),
                        ALU.mult))
                    t_m = bump("vector", "ve")
                    E("vector", lambda h, ts0=ts0: h.tensor_tensor(
                        uu[:],
                        xdt[:, :, ts0:ts0 + TC].rearrange("p c t -> p t c")[:, :, :, None]
                        .broadcast_to([128, TC, NCT, N]),
                        brep[:, :, ts0:ts0 + TC].rearrange("p n t -> p t n")[:, :, None, :]
                        .broadcast_to([128, TC, NCT, N]),
                        ALU.mult))
                    W("scalar", "ve", t_m)
                    E("scalar", lambda h: h.activation(flat(da[:]), flat(da[:]), ACT.Exp))
                    t_exp = bump("scalar", "se")
                    W("vector", "se", t_exp)
                    for j in range(TC):
                        prev = sh[:, TC] if j == 0 else sh[:, j]
                        E("vector", lambda h, j=j, prev=prev: h.scalar_tensor_tensor(
                            st1, flat(da[:, j]), EM10, flat(prev), ALU.max, ALU.mult))
                        E("vector", lambda h, j=j: h.tensor_tensor(
                            flat(sh[:, j + 1]), st1, flat(uu[:, j]), ALU.add))
                        E("vector", lambda h, j=j: h.tensor_scalar(
                            flat(sh[:, j + 1]), flat(sh[:, j + 1]), 10.0, -10.0, ALU.min, ALU.max))
                    E("vector", lambda h, ts0=ts0: h.tensor_tensor(
                        uu[:], sh[:, 1:TC + 1],
                        crep[:, :, ts0:ts0 + TC].rearrange("p n t -> p t n")[:, :, None, :]
                        .broadcast_to([128, TC, NCT, N]),
                        ALU.mult))
                    E("vector", lambda h, ts0=ts0: h.tensor_reduce(
                        yt[:, :, ts0:ts0 + TC].rearrange("p c t -> p t c"),
                        uu[:], AX.X, ALU.add))
                    bump("vector", "ve")

                for c in range(NCT):
                    E("vector", lambda h, c=c: h.scalar_tensor_tensor(
                        yt[:, c], xcs[:, c], dmat[:, c:c + 1], yt[:, c], ALU.mult, ALU.add))
                E("vector", lambda h: h.tensor_tensor(gtb[:], flat(yt[:]), flat(sgz[:]), ALU.mult))
                t_g = bump("vector", "ve")

                gtb3 = gtb.rearrange("p (c t) -> p c t", c=NCT)
                W("tensor", "ve", t_g)
                W("tensor", "ve", t_ln_prev)
                for c in range(NCT):
                    twc = dma("sync", wob[:, c % 2], wob_dram[128 * c:128 * (c + 1), :], "wo")
                    W("tensor", "wo", twc)
                    t_wo = te_group([
                        lambda h, c=c: h.matmul(
                            pb0[:, 0:512], gtb3[:, c], wob[:, c % 2, 0:512],
                            start=(c == 0), stop=(c == NCT - 1)),
                        lambda h, c=c: h.matmul(
                            pb1[:, 0:512], gtb3[:, c], wob[:, c % 2, 512:1024],
                            start=(c == 0), stop=(c == NCT - 1))])
                    W("sync", "te", t_wo)

                W("vector", "te", t_wo)
                W("vector", "spx", t_spill_prev)
                W("vector", "xt", ttk)
                E("vector", lambda h: h.tensor_tensor(lnv[:, 0:512], pb0[:], xtk[:, 0:512], ALU.add))
                E("vector", lambda h: h.tensor_tensor(lnv[:, 512:1024], pb1[:], xtk[:, 512:1024], ALU.add))
                E("vector", lambda h: h.tensor_reduce(cstat[:, 0:1], lnv[:], AX.X, ALU.add))
                E("vector", lambda h: h.tensor_scalar(cstat[:, 1:2], cstat[:, 0:1], 1.0 / D, None, ALU.mult))
                E("vector", lambda h: h.tensor_scalar(lnv[:], lnv[:], cstat[:, 1:2], None, ALU.subtract))
                t_ctr = bump("vector", "ve")
                t_ln_prev = t_ctr
                t_ctr_prev = t_ctr
                W("scalar", "ve", t_ctr)
                E("scalar", lambda h: h.activation(lnq, lnv[:], ACT.Square, accum_out=cstat[:, 2:3]))
                t_sq = bump("scalar", "se")
                W("vector", "se", t_sq)
                E("vector", lambda h: h.tensor_scalar(cstat[:, 3:4], cstat[:, 2:3], 1.0 / D, LN_EPS, ALU.mult, ALU.add))
                E("vector", lambda h: h.reciprocal(cstat[:, 4:5], cstat[:, 3:4]))
                t_rc = bump("vector", "ve")
                W("scalar", "ve", t_rc)
                E("scalar", lambda h: h.activation(cstat[:, 5:6], cstat[:, 4:5], ACT.Sqrt))
                t_rs = bump("scalar", "se")
                W("vector", "se", t_rs)
                E("vector", lambda h: h.tensor_scalar(lnv[:], lnv[:], cstat[:, 5:6], None, ALU.mult))
                E("vector", lambda h: h.tensor_reduce(cstat[:, 6:7], lnv[:], AX.X, ALU.max))
                E("vector", lambda h: h.tensor_tensor(xmax[:], xmax[:], cstat[:, 6:7], ALU.max))
                t_x1 = bump("vector", "ve")
                W("gpsimd", "ve", t_x1)
                t_spill_prev = dma("gpsimd", x1_dram[t0:t0 + TT, :], lnv[:], "spx")

            mamba_ve = P.cnt.get("ve", 0)
            if not phase_done and phase in ("tile0", "mamba"):
                W("gpsimd", "ve", mamba_ve)
                tst = dma("gpsimd", out[0:TT, :], lnv[:], "st")
                W("gpsimd", "st", P.cnt["st"])
                phase_done = True

            W("gpsimd", "ve", mamba_ve)
            tsp = dma("gpsimd", vec_dram[0:1, 0:128].rearrange("o p -> p o"), xmax[:], "vb")
            W("gpsimd", "vb", tsp)
            tg = dma("gpsimd", sml[:], vec_dram[0:1, 0:128], "vb")
            W("vector", "vb", tg)
            E("vector", lambda h: h.tensor_reduce(smlb[0:1, 0:1], sml[:], AX.X, ALU.max))
            tv = bump("vector", "ve")
            W("gpsimd", "ve", tv)
            tsp = dma("gpsimd", cc1_in, smlb[:], "vb")
            W("gpsimd", "vb", tsp)
            tcc = INC("gp", 1)
            E("gpsimd", lambda h: h.collective_compute(
                "AllReduce", ALU.max, replica_groups=[list(range(8))],
                ins=[cc1_in], outs=[cc1_out]).then_inc(s_gp, 1))

            W("sync", "ve", mamba_ve)
            for fc in range(NF):
                t = dma("sync", stg[:, 0:D], w1[128 * fc:128 * (fc + 1), :], "stg")
                W("vector", "stg", t)
                E("vector", lambda h, fc=fc: h.tensor_reduce(
                    sw1[:, fc:fc + 1], stg[:, 0:D], AX.X, ALU.max, apply_absolute_value=True))
                tv = bump("vector", "ve")
                W("sync", "ve", tv)
            vpad()
            E("vector", lambda h: h.tensor_scalar(sw1[:], sw1[:], EPS, None, ALU.max))
            recip_nr(qK[:, 0:NF], sw1[:], qK[:, NF:2 * NF])
            vpad()
            E("vector", lambda h: h.tensor_scalar(qK[:, 0:NF], qK[:, 0:NF], 127.0, None, ALU.mult))
            E("vector", lambda h: h.tensor_scalar(sw1[:], sw1[:], 1.0 / 127.0, None, ALU.mult))

            W("gpsimd", "gp", tcc)
            tg = dma("gpsimd", scal[:, 0:1], cc1_out[0:1, 0:1].partition_broadcast(128), "vb")
            W("vector", "vb", tg)
            E("vector", lambda h: h.tensor_scalar(scal[:, 1:2], scal[:, 0:1], EPS, None, ALU.max))
            recip_nr(scal[:, 2:3], scal[:, 1:2], scal[:, 8:9])
            vpad()
            E("vector", lambda h: h.tensor_scalar(scal[:, 2:3], scal[:, 2:3], 255.0, None, ALU.mult))
            E("vector", lambda h: h.tensor_scalar(scal[:, 3:4], scal[:, 1:2], 1.0 / 255.0, None, ALU.mult))
            vpad()
            E("vector", lambda h: h.tensor_scalar(sprod1[:], sw1[:], scal[:, 3:4], None, ALU.mult))
            recip_nr(qR[:, 0:NF], sprod1[:], qR[:, NF:2 * NF])
            tv = bump("vector", "ve")
            tld = dma("sync", qC[:, 0:NF], b1, "stg")
            W("vector", "stg", tld)
            vpad()
            E("vector", lambda h: h.tensor_tensor(bq1[:], qC[:, 0:NF], qR[:, 0:NF], ALU.mult))
            round_chain(bq1[:], qC[:, 0:NF], qR[:, 0:NF], qC[:, NF:2 * NF], pads=True)
            vpad()
            E("vector", lambda h: h.tensor_tensor(bq1[:], bq1[:], sprod1[:], ALU.mult))
            tv = bump("vector", "ve")
            W("gpsimd", "ve", tv)
            for fc in range(NF):
                tsp = dma("gpsimd", vec_dram[1:2, 128 * fc:128 * (fc + 1)].rearrange("o p -> p o"),
                          qK[:, fc:fc + 1], "vb")
            W("gpsimd", "vb", tsp)
            t_rep1 = dma("gpsimd", qRep[:],
                         vec_dram[1:2, 0:NF * 128].partition_broadcast(128), "vb")

            for dc in range(ND):
                t = dma("sync", stg, w1t[128 * dc:128 * (dc + 1), :], "stg")
                W("vector", "stg", t)
                if dc == 0:
                    W("vector", "vb", t_rep1)
                E("vector", lambda h: h.tensor_tensor(qV[:], qV[:], qRep[:], ALU.mult))
                round_chain(qV[:], qK[:], qR[:], qC[:])
                E("vector", lambda h: h.tensor_scalar(qV[:], qV[:], -128.0, 127.0, ALU.max, ALU.min))
                E("vector", lambda h, dc=dc: h.tensor_copy(winb3[:, dc, 0:FF], qV[:]))
                tv = bump("vector", "ve")
                W("sync", "ve", tv)

            W("sync", "spx", P.cnt.get("spx", 0))
            t_lnv_free = 0
            t_spq_all = 0
            t_tt_prev = 0
            t_round_prev = 0
            for i in range(NTILE):
                t0 = i * TT
                W("sync", "se", t_lnv_free)
                t = dma("sync", lnv[:], x1_dram[t0:t0 + TT, :], "stg")
                W("scalar", "stg", t)
                W("scalar", "te", t_tt_prev)
                W("scalar", "ve", t_round_prev)
                E("scalar", lambda h: h.activation(qV[:, 0:D], lnv[:], ACT.Identity,
                                                   scale=scal[:, 2:3]))
                E("scalar", lambda h: h.activation(qK[:, 0:D], qV[:, 0:D], ACT.Copy, bias=MAGIC))
                E("scalar", lambda h: h.activation(qK[:, 0:D], qK[:, 0:D], ACT.Copy, bias=-MAGIC))
                t_lnv_free = bump("scalar", "se")
                W("vector", "se", t_lnv_free)
                round_tail(qV[:, 0:D], qK[:, 0:D], qR[:, 0:D], qC[:, 0:D])
                E("vector", lambda h: h.tensor_scalar(qV[:, 0:D], qV[:, 0:D], 0.0, 255.0, ALU.max, ALU.min))
                t_round_prev = bump("vector", "ve")
                W("tensor", "ve", t_round_prev)
                gq = qV[:, 0:D].rearrange("p (c w) -> p c w", c=ND)
                gtb4 = gtb[:, 0:1024].rearrange("p (g c t) -> p g c t", g=2, c=4)
                for gb in range(2):
                    bank = (pb2, pb3)[gb]
                    tt = te_group([
                        lambda h, dcc=dcc, bank=bank, k=k: h.transpose(
                            bank[:, 128 * k:128 * (k + 1)], gq[:, dcc], identb[:])
                        for k, dcc in enumerate(range(4 * gb, 4 * gb + 4))])
                    W("vector", "te", tt)
                    W("vector", "spq", t_spq_all)
                    E("vector", lambda h, gb=gb, bank=bank: h.tensor_copy(
                        flat(gtb4[:, gb]), bank[:, 0:512]))
                    tv2 = bump("vector", "ve")
                    W("gpsimd", "ve", tv2)
                    dma("gpsimd",
                        x1q_dram[512 * gb:512 * (gb + 1), t0:t0 + TT]
                        .rearrange("(c p) t -> p c t", p=128),
                        gtb4[:, gb], "spq")
                t_spq_all = P.cnt["spq"]
                t_tt_prev = tt

            pool = [gtb, flat(sgz[:])]
            t_hq_prev = [0, 0]
            t_f_last = 0
            t_h_prev = 0
            t_hv_prev = 0
            W("sync", "spq", P.cnt.get("spq", 0))
            W("sync", "ve", P.cnt.get("ve", 0))
            W("sync", "te", P.cnt.get("te", 0))
            for tb in range(4):
                tb0 = tb * 512
                if tb > 0:
                    W("sync", "te", t_f_last)
                for dc in range(ND):
                    dst = pool[dc // 4][:, 512 * (dc % 4):512 * (dc % 4 + 1)]
                    t_rhs = dma("sync", dst, x1q_dram[128 * dc:128 * (dc + 1), tb0:tb0 + 512], "rhs")
                W("tensor", "rhs", t_rhs)
                for fc in range(NF):
                    bank = (pa0, pa1)[fc % 2]
                    qsl = qV[:, 512 * (fc % 2):512 * (fc % 2) + 512]
                    t_f = te_group([lambda h, fc=fc, dc=dc, bank=bank,
                                    src=pool[dc // 4][:, 512 * (dc % 4):512 * (dc % 4 + 1)]:
                                    h.matmul(bank[:], winb3[:, dc, 128 * fc:128 * (fc + 1)], src,
                                             start=(dc == 0), stop=(dc == ND - 1))
                                    for dc in range(ND)])
                    W("scalar", "te", t_f)
                    W("scalar", "sph" if fc % 2 == 0 else "sph2", t_hq_prev[fc % 2])
                    W("scalar", "ve", t_hv_prev)
                    E("scalar", lambda h, fc=fc, bank=bank, qsl=qsl: h.activation(
                        qsl, bank[:], ACT.Identity,
                        bias=bq1[:, fc:fc + 1], scale=sprod1[:, fc:fc + 1]))
                    t_h = bump("scalar", "se")
                    W("vector", "se", t_h)
                    E("vector", lambda h, qsl=qsl: h.tensor_reduce(cstat[:, 7:8], qsl, AX.X, ALU.max))
                    vpad()
                    E("vector", lambda h: h.tensor_tensor(hmax[:], hmax[:], cstat[:, 7:8], ALU.max))
                    t_hv = bump("vector", "ve")
                    W("gpsimd", "se", t_h)
                    t_hq_prev[fc % 2] = dma(
                        "gpsimd", h_dram[128 * fc:128 * (fc + 1), tb0:tb0 + 512],
                        qsl, "sph" if fc % 2 == 0 else "sph2")
                    W("tensor", "se", t_h_prev)
                    t_h_prev = t_h
                    t_hv_prev = t_hv
                    t_f_last = t_f

            hv = P.cnt["ve"]
            W("gpsimd", "ve", hv)
            W("gpsimd", "sph", t_hq_prev[0])
            W("gpsimd", "sph2", t_hq_prev[1])
            tsp = dma("gpsimd", vec_dram[2:3, 0:128].rearrange("o p -> p o"), hmax[:], "vb")
            W("gpsimd", "vb", tsp)
            tg = dma("gpsimd", sml[:], vec_dram[2:3, 0:128], "vb")
            W("vector", "vb", tg)
            E("vector", lambda h: h.memset(smlb[:], NEG_BIG))
            vpad()
            E("vector", lambda h: h.tensor_reduce(smlb[0:1, 0:1], sml[:], AX.X, ALU.max))
            tv = bump("vector", "ve")
            W("gpsimd", "ve", tv)
            tsp = dma("gpsimd", cc2_in, smlb[:], "vb")
            W("gpsimd", "vb", tsp)
            tcc = INC("gp", 1)
            E("gpsimd", lambda h: h.collective_compute(
                "AllReduce", ALU.max, replica_groups=[list(range(8))],
                ins=[cc2_in], outs=[cc2_out]).then_inc(s_gp, 1))

            W("sync", "se", P.cnt.get("se", 0))
            W("sync", "ve", P.cnt.get("ve", 0))
            W("sync", "sph", P.cnt.get("sph", 0))
            W("sync", "sph2", P.cnt.get("sph2", 0))
            for dc in range(ND):
                t = dma("sync", stg, w2[128 * dc:128 * (dc + 1), :], "stg")
                W("vector", "stg", t)
                E("vector", lambda h, dc=dc: h.tensor_reduce(
                    sw2[:, dc:dc + 1], stg, AX.X, ALU.max, apply_absolute_value=True))
                tv = bump("vector", "ve")
                W("sync", "ve", tv)
            vpad()
            E("vector", lambda h: h.tensor_scalar(sw2[:], sw2[:], EPS, None, ALU.max))
            recip_nr(qK[:, 0:ND], sw2[:], qK[:, ND:2 * ND])
            vpad()
            E("vector", lambda h: h.tensor_scalar(qK[:, 0:ND], qK[:, 0:ND], 127.0, None, ALU.mult))
            E("vector", lambda h: h.tensor_scalar(sw2[:], sw2[:], 1.0 / 127.0, None, ALU.mult))

            W("gpsimd", "gp", tcc)
            tg = dma("gpsimd", scal[:, 4:5], cc2_out[0:1, 0:1].partition_broadcast(128), "vb")
            W("vector", "vb", tg)
            E("vector", lambda h: h.tensor_scalar(scal[:, 5:6], scal[:, 4:5], EPS, None, ALU.max))
            recip_nr(scal[:, 6:7], scal[:, 5:6], scal[:, 9:10])
            vpad()
            E("vector", lambda h: h.tensor_scalar(scal[:, 6:7], scal[:, 6:7], 255.0, None, ALU.mult))
            E("vector", lambda h: h.tensor_scalar(scal[:, 7:8], scal[:, 5:6], 1.0 / 255.0, None, ALU.mult))
            vpad()
            E("vector", lambda h: h.tensor_scalar(sw2[:], sw2[:], scal[:, 7:8], None, ALU.mult))
            recip_nr(qR[:, 0:ND], sw2[:], qR[:, ND:2 * ND])
            tv = bump("vector", "ve")
            tld = dma("sync", qC[:, 0:ND], b2, "stg")
            W("vector", "stg", tld)
            vpad()
            E("vector", lambda h: h.tensor_tensor(qC[:, ND:2 * ND], qC[:, 0:ND], qR[:, 0:ND], ALU.mult))
            round_chain(qC[:, ND:2 * ND], qK[:, NF:NF + ND], qR[:, ND:2 * ND], qC[:, 2 * ND:3 * ND], pads=True)
            vpad()
            E("vector", lambda h: h.tensor_tensor(qC[:, ND:2 * ND], qC[:, ND:2 * ND], sw2[:], ALU.mult))
            tv = bump("vector", "ve")
            W("gpsimd", "ve", tv)
            for dc in range(ND):
                tsp = dma("gpsimd", vec_dram[3:4, 128 * dc:128 * (dc + 1)].rearrange("o p -> p o"),
                          qC[:, ND + dc:ND + dc + 1], "vb")
                tsp = dma("gpsimd", vec_dram[4:5, 128 * dc:128 * (dc + 1)].rearrange("o p -> p o"),
                          sw2[:, dc:dc + 1], "vb")
                tsp = dma("gpsimd", vec_dram[5:6, 128 * dc:128 * (dc + 1)].rearrange("o p -> p o"),
                          qK[:, dc:dc + 1], "vb")
            W("gpsimd", "vb", tsp)
            tg = dma("gpsimd", repv, vec_dram[3:4, 0:D].partition_broadcast(128), "vb")
            tg = dma("gpsimd", repv2, vec_dram[4:5, 0:D].partition_broadcast(128), "vb")
            tg = dma("gpsimd", qRep[:, 0:D], vec_dram[5:6, 0:D].partition_broadcast(128), "vb")
            t_rep2 = tg

            w2qt = winb.rearrange("p (fc w) -> p fc w", fc=NF)
            for fc in range(NF):
                t = dma("sync", stg[:, 0:D], w2t[128 * fc:128 * (fc + 1), :], "stg")
                W("vector", "stg", t)
                if fc == 0:
                    W("vector", "vb", t_rep2)
                E("vector", lambda h: h.tensor_tensor(qV[:, 0:D], qV[:, 0:D], qRep[:, 0:D], ALU.mult))
                round_chain(qV[:, 0:D], qK[:, 0:D], qR[:, 0:D], qC[:, 0:D])
                E("vector", lambda h: h.tensor_scalar(qV[:, 0:D], qV[:, 0:D], -128.0, 127.0, ALU.max, ALU.min))
                E("vector", lambda h, fc=fc: h.tensor_copy(w2qt[:, fc, 0:D], qV[:, 0:D]))
                tv = bump("vector", "ve")
                W("sync", "ve", tv)

            t_xq_prev = 0
            t_rnd_prev = 0
            t_rnd = {-2: 0, -1: 0}
            xqbuf = [qV[:, 0:T], qRep[:, 2048:4096]]
            tld_h = {}

            def issue_h(fc):
                if fc >= NF:
                    return
                W("sync", "ve", t_rnd[fc - 2])
                tld_h[fc] = dma("sync", xqbuf[fc % 2],
                                h_dram[128 * fc:128 * (fc + 1), :],
                                "stg" if fc % 2 == 0 else "xf")

            issue_h(0)
            issue_h(1)
            for fc in range(NF):
                buf = xqbuf[fc % 2]
                W("scalar", "stg" if fc % 2 == 0 else "xf", tld_h[fc])
                W("scalar", "ve", t_rnd_prev)
                E("scalar", lambda h, buf=buf: h.activation(buf, buf, ACT.Identity,
                                                            scale=scal[:, 6:7]))
                E("scalar", lambda h, buf=buf: h.activation(qK[:, 0:T], buf, ACT.Copy, bias=MAGIC))
                E("scalar", lambda h: h.activation(qK[:, 0:T], qK[:, 0:T], ACT.Copy, bias=-MAGIC))
                ts_sc = bump("scalar", "se")
                W("vector", "se", ts_sc)
                W("vector", "spq", t_xq_prev)
                round_tail(buf, qK[:, 0:T], qR[:, 0:T], qC[:, 0:T])
                E("vector", lambda h, buf=buf: h.tensor_scalar(buf, buf, 0.0, 255.0, ALU.max, ALU.min))
                E("vector", lambda h, buf=buf: h.tensor_copy(gtb[:], buf))
                t_rnd_prev = bump("vector", "ve")
                t_rnd[fc] = t_rnd_prev
                W("gpsimd", "ve", t_rnd_prev)
                t_xq_prev = dma("gpsimd", xq2_dram[128 * fc:128 * (fc + 1), :], gtb[:], "spq")
                issue_h(fc + 2)

            W("sync", "spq", P.cnt.get("spq", 0))
            grp = [gtb.rearrange("p (c t) -> p c t", c=NCT)[:, 0:8],
                   gtb.rearrange("p (c t) -> p c t", c=NCT)[:, 8:16],
                   sgz[:, 0:8], sgz[:, 8:16]]
            NG = NF // 8
            t_g_used = [0, 0, 0, 0]
            t_g_load = {}
            gseq = 0
            t_lnv2_free = 0
            t_xtk_free = 0
            for tb in range(NTILE):
                tb0 = tb * TT
                W("sync", "ve", t_xtk_free)
                tld1 = dma("sync", xtk[:], x1_dram[tb0:tb0 + TT, :], "xt")
                for g in range(NG):
                    gi = (gseq + g) % 4
                    W("sync", "te", t_g_used[gi])
                    t_g_load[gseq + g] = dma(
                        "sync", grp[gi],
                        xq2_dram[1024 * g:1024 * (g + 1), tb0:tb0 + TT]
                        .rearrange("(c p) t -> p c t", p=128), "qc")
                W("tensor", "qc", t_g_load[gseq + NG - 1])
                for g in range(NG):
                    gi = (gseq + g) % 4
                    if g == 0:
                        W("tensor", "ve", t_lnv2_free)
                    for cc in range(8):
                        fc = g * 8 + cc
                        tte = te_group([
                            lambda h, fc=fc, gi=gi, cc=cc: h.matmul(
                                pb0[:, 0:512], grp[gi][:, cc], w2qt[:, fc, 0:512],
                                start=(fc == 0), stop=(fc == NF - 1)),
                            lambda h, fc=fc, gi=gi, cc=cc: h.matmul(
                                pb1[:, 0:512], grp[gi][:, cc], w2qt[:, fc, 512:1024],
                                start=(fc == 0), stop=(fc == NF - 1))])
                    t_g_used[gi] = tte
                gseq += NG
                W("vector", "te", tte)
                W("vector", "xt", tld1)
                E("vector", lambda h: h.tensor_tensor(lnv[:, 0:512], pb0[:], repv2[:, 0:512], ALU.mult))
                E("vector", lambda h: h.tensor_tensor(lnv[:, 512:1024], pb1[:], repv2[:, 512:1024], ALU.mult))
                E("vector", lambda h: h.tensor_tensor(lnv[:], lnv[:], repv, ALU.add))
                E("vector", lambda h: h.tensor_tensor(lnv[:], lnv[:], xtk[:], ALU.add))
                E("vector", lambda h: h.tensor_reduce(cstat[:, 0:1], lnv[:], AX.X, ALU.add))
                vpad()
                E("vector", lambda h: h.tensor_scalar(cstat[:, 1:2], cstat[:, 0:1], 1.0 / D, None, ALU.mult))
                vpad()
                E("vector", lambda h: h.tensor_scalar(lnv[:], lnv[:], cstat[:, 1:2], None, ALU.subtract))
                t_ctr = bump("vector", "ve")
                t_xtk_free = t_ctr
                t_lnv2_free = t_ctr
                W("scalar", "ve", t_ctr)
                E("scalar", lambda h: h.activation(lnq2, lnv[:], ACT.Square, accum_out=cstat[:, 2:3]))
                t_sq = bump("scalar", "se")
                W("vector", "se", t_sq)
                E("vector", lambda h: h.tensor_scalar(cstat[:, 3:4], cstat[:, 2:3], 1.0 / D, LN_EPS, ALU.mult, ALU.add))
                rsqrt_nr(cstat[:, 5:6], cstat[:, 3:4], cstat[:, 4:5])
                vpad()
                E("vector", lambda h: h.tensor_scalar(lnv[:], lnv[:], cstat[:, 5:6], None, ALU.mult))
                tv = bump("vector", "ve")
                W("gpsimd", "ve", tv)
                tst = dma("gpsimd", out[tb0:tb0 + TT, :], lnv[:], "st")
                W("vector", "st", tst)

            if dbg_aps:
                for nm, src_ap in [("dbg_scal", scal[:, 0:8]), ("dbg_sprod1", sprod1[:]),
                                   ("dbg_bq1", bq1[:]), ("dbg_rep", repv),
                                   ("dbg_rep2", repv2), ("dbg_sw1", sw1[:])]:
                    if nm in dbg_aps:
                        dma("gpsimd", dbg_aps[nm], src_ap, "st")
                if "dbg_x1" in dbg_aps:
                    dma("gpsimd", dbg_aps["dbg_x1"], x1_dram, "st")
                if "dbg_h" in dbg_aps:
                    dma("gpsimd", dbg_aps["dbg_h"], h_dram, "st")
            W("gpsimd", "st", P.cnt["st"])

        @blk.sync
        def _(h):
            P.run("sync", h)

        @blk.gpsimd
        def _(h):
            P.run("gpsimd", h)

        @blk.vector
        def _(h):
            P.run("vector", h)

        @blk.scalar
        def _(h):
            P.run("scalar", h)

        @blk.tensor
        def _(h):
            P.run("tensor", h)

    return nc


def _prep_core_inputs(b, x, W_in, W_conv, b_conv_, W_xproj, W_dt, b_dt_, A_log, D_, W_out,
                      w1_, b1_, w2_, b2_):
    xt = np.ascontiguousarray(x[b], dtype=np.float32)
    return {
        "x_tok": xt,
        "x_feat": np.ascontiguousarray(xt.T),
        "w_in": W_in,
        "w_conv": np.ascontiguousarray(
            W_conv.T.reshape(NCT, 128, 4).transpose(1, 0, 2).reshape(128, NCT * 4)),
        "b_conv": np.ascontiguousarray(b_conv_.reshape(NCT, 128).T),
        "w_xp": W_xproj,
        "w_dt": W_dt,
        "b_dt": np.ascontiguousarray(b_dt_.reshape(NCT, 128).T),
        "a_log": np.ascontiguousarray(
            A_log.reshape(NCT, 128, N).transpose(1, 0, 2).reshape(128, NCT * N)),
        "d_vec": np.ascontiguousarray(D_.reshape(NCT, 128).T),
        "w_out": W_out,
        "w1": w1_,
        "w1t": np.ascontiguousarray(w1_.T),
        "b1": np.ascontiguousarray(b1_.reshape(NF, 128).T),
        "w2": w2_,
        "w2t": np.ascontiguousarray(w2_.T),
        "b2": np.ascontiguousarray(b2_.reshape(ND, 128).T),
        "ident": np.eye(128, dtype=np.float32),
    }


def kernel(L0_attn_x, L0_attn_W_in, L0_attn_W_conv, L0_attn_b_conv, L0_attn_W_xproj,
           L0_attn_W_dt, L0_attn_b_dt, L0_attn_A_log, L0_attn_D, L0_attn_W_out,
           L0_ffn_w1, L0_ffn_b1, L0_ffn_w2, L0_ffn_b2):
    args = [np.ascontiguousarray(np.asarray(a), dtype=np.float32) for a in
            (L0_attn_x, L0_attn_W_in, L0_attn_W_conv, L0_attn_b_conv, L0_attn_W_xproj,
             L0_attn_W_dt, L0_attn_b_dt, L0_attn_A_log, L0_attn_D, L0_attn_W_out,
             L0_ffn_w1, L0_ffn_b1, L0_ffn_w2, L0_ffn_b2)]
    nc = build_nc()
    in_maps = [_prep_core_inputs(b, args[0], *args[1:]) for b in range(B)]
    res = run_bass_kernel_spmd(nc, in_maps, core_ids=list(range(B)))
    return np.stack([res.results[b]["out"] for b in range(B)], axis=0)
